# version 08
# baseline (speedup 1.0000x reference)
import numpy as np
import concourse.bass as bass
import concourse.mybir as mybir
from concourse.bass_utils import run_bass_kernel_spmd
from concourse.bass_core_rust import IndirectOffsetOnAxis

F32 = mybir.dt.float32
F16 = mybir.dt.float16
I32 = mybir.dt.int32
AF = mybir.ActivationFunctionType
ALU = mybir.AluOpType
AXX = mybir.AxisListType.X

NCORES = 8
L = 4096
B_ROWS = 4
ROWS_TOTAL = B_ROWS * L
ROWS_CORE = ROWS_TOTAL // NCORES
NT = ROWS_CORE // 128
ST = 4
NST = NT // ST
QMAX = 127.0
EPS = 1e-5
MAGIC = float(1.5 * 2 ** 23)
FS = 0.125
IFS = 1.0 / FS
IFS2 = IFS * IFS



def _host_consts():
    c = {}
    a = np.arange(128)
    b32 = np.arange(32)

    th = (np.outer(a, a) / 128.0)[None, :, :] + \
         (b32[:, None, None] * a[None, None, :] / 4096.0)
    ang = 2 * np.pi * th
    c["Gr"] = (FS * np.cos(ang)).astype(np.float16)
    c["Gi"] = (-FS * np.sin(ang)).astype(np.float16)

    bd = np.outer(b32, b32) / 32.0
    f2r = np.cos(2 * np.pi * bd)
    f2i = -np.sin(2 * np.pi * bd)

    def struct(m):
        out = np.zeros((128, 128), np.float64)
        for rp in range(4):
            out[rp::4, rp::4] = m
        return out

    c["F2r"] = struct(f2r).astype(np.float16)
    c["F2i"] = struct(f2i).astype(np.float16)
    c["F2ni"] = struct(-f2i).astype(np.float16)

    fbr = np.cos(2 * np.pi * bd)
    fbi = np.sin(2 * np.pi * bd)
    c["FBr"] = struct(fbr).astype(np.float16)
    c["FBi"] = struct(fbi).astype(np.float16)
    c["FBni"] = struct(-fbi).astype(np.float16)

    th2 = (np.outer(a, a) / 128.0)[None, :, :] + \
          (b32[:, None, None] * a[None, :, None] / 4096.0)
    ang2 = 2 * np.pi * th2
    c["Hr"] = (np.cos(ang2) / 4096.0).astype(np.float16)
    c["Hni"] = (-np.sin(ang2) / 4096.0).astype(np.float16)

    c["ident"] = np.eye(128, dtype=np.float16)
    return c




class Buf:
    __slots__ = ("w", "r")

    def __init__(self):
        self.w = None
        self.r = {}

    def read_deps(self):
        return [self.w] if self.w else []

    def write_deps(self):
        d = [(e, c) for (e, c) in self.r.items()]
        if self.w:
            d.append(self.w)
        return d

    def note_read(self, tok):
        e, c = tok
        if self.r.get(e, 0) < c:
            self.r[e] = c

    def note_write(self, tok):
        self.w = tok
        self.r = {}


def build_program():
    nc = bass.Bass("TRN2", target_bir_lowering=False)

    x_in = nc.dram_tensor("x_in", [ROWS_CORE, L], F32, kind="ExternalInput").ap()
    w_in = nc.dram_tensor("w_in", [L, L], F32, kind="ExternalInput").ap()
    out_t = nc.dram_tensor("out", [ROWS_CORE, L], F32, kind="ExternalOutput").ap()
    cns = {}
    cshapes = {"Gr": [32, 128, 128], "Gi": [32, 128, 128],
               "F2r": [128, 128], "F2i": [128, 128], "F2ni": [128, 128],
               "FBr": [128, 128], "FBi": [128, 128], "FBni": [128, 128],
               "Hr": [32, 128, 128], "Hni": [32, 128, 128],
               "ident": [128, 128]}
    for k, shp in cshapes.items():
        cns[k] = nc.dram_tensor("c_" + k, shp, F16, kind="ExternalInput").ap()
    wq_scr = nc.dram_tensor("wq_scr", [32, 128, L], F16, kind="Internal").ap()
    x1_scr = nc.dram_tensor("x1_scr", [ROWS_CORE, L], F32, kind="Internal").ap()
    ffn_scr = nc.dram_tensor("ffn_scr", [NT, 128, L], F16, kind="Internal").ap()
    wsf_hbm = nc.dram_tensor("wsf_hbm", [1, L], F32, kind="Internal").ap()
    lnx_scr = nc.dram_tensor("lnx_scr", [128, 2 * L], F32, kind="Internal").ap()
    pbase_in = nc.dram_tensor("pbase", [128, 1], I32, kind="ExternalInput").ap()

    def sb(name, shape, dtype):
        return nc.alloc_sbuf_tensor(name, shape, dtype)

    sGr = sb("sGr", [128, 32 * 128], F16)
    sGi = sb("sGi", [128, 32 * 128], F16)
    sHr = sb("sHr", [128, 32 * 128], F16)
    sHni = sb("sHni", [128, 32 * 128], F16)
    sF2r = sb("sF2r", [128, 128], F16)
    sF2i = sb("sF2i", [128, 128], F16)
    sF2ni = sb("sF2ni", [128, 128], F16)
    sFBr = sb("sFBr", [128, 128], F16)
    sFBi = sb("sFBi", [128, 128], F16)
    sFBni = sb("sFBni", [128, 128], F16)
    sIdent = sb("sIdent", [128, 128], F16)

    s_ln1 = sb("s_ln1", [128, L], F16)
    s_mt = sb("s_mt", [128, L], F16)
    s_p2r = sb("s_p2r", [128, L], F16)
    s_y2 = sb("s_y2", [128, L], F16)
    s_cr = sb("s_cr", [128, L], F16)
    s_ci = sb("s_ci", [128, L], F16)
    s_xr = sb("s_xr", [128, L], F16)
    s_xi = sb("s_xi", [128, L], F16)
    s_yt = sb("s_yt", [128, L], F16)

    sA = sb("sA", [128, L], F32)
    sB = sb("sB", [128, L], F32)
    sC = sb("sC", [128, L], F32)

    xqT = sb("xqT", [128, ST, L], F16)
    wst = sb("wst", [128, 4, 512], F16)
    wsf = sb("wsf", [128, L], F32)
    asc = sb("asc", [128, ST], F32)

    st1 = sb("st1", [128, 16], F32)
    tops = sb("tops", [128, 8], F32)
    wmax = sb("wmax", [128, 32], F32)
    dum = sb("dum", [128, 4], F32)
    s8v = sb("s8v", [128, 8], F32)
    tid = sb("tid", [128, 8], mybir.dt.uint32)
    tix = sb("tix", [128, 2], I32)
    pbs = sb("pbs", [128, 1], I32)

    pt0 = nc.alloc_psum_tensor("pt0", [128, 512], F16)
    pt1 = nc.alloc_psum_tensor("pt1", [128, 512], F16)
    pm0 = nc.alloc_psum_tensor("pm0", [128, 512], F32)
    pm1 = nc.alloc_psum_tensor("pm1", [128, 512], F32)
    pa = [nc.alloc_psum_tensor("pa%d" % i, [128, 512], F32) for i in range(4)]

    COMPUTE = ("te", "ve", "ac")
    DMA_SIGS = ["sd", "gd", "cl", "xl", "wl", "wqs", "wb", "lx", "gt",
                "xs", "fs", "ef", "ex", "os", "w0", "w1", "w2", "w3"]
    sem_names = list(COMPUTE) + DMA_SIGS
    sems = {n: nc.alloc_semaphore(n + "_sem") for n in sem_names}
    STEP = {n: (1 if n in COMPUTE else 16) for n in sem_names}

    sched = []

    def step(eng, fn, reads=(), writes=(), sig=None):
        sched.append((eng, fn, tuple(reads), tuple(writes),
                      sig if sig is not None else eng))

    E = {}

    def cp(eng, out_ap, in_ap, reads, writes):
        if eng == "ve":
            step("ve", lambda o=out_ap, i=in_ap: E["ve"].tensor_copy(out=o, in_=i),
                 reads=reads, writes=writes)
        else:
            step("ac", lambda o=out_ap, i=in_ap: E["ac"].copy(out=o, in_=i),
                 reads=reads, writes=writes)

    def ts(col_out, col_in, const, op=ALU.mult):
        step("ve", lambda: E["ve"].tensor_scalar(
            out=st1.ap()[:, col_out:col_out + 1],
            in0=st1.ap()[:, col_in:col_in + 1],
            scalar1=float(const), scalar2=None, op0=op),
            reads=["st"], writes=["st"])

    def tt(col_out, col_a, col_b, op=ALU.mult):
        step("ve", lambda: E["ve"].tensor_tensor(
            out=st1.ap()[:, col_out:col_out + 1],
            in0=st1.ap()[:, col_a:col_a + 1],
            in1=st1.ap()[:, col_b:col_b + 1], op=op),
            reads=["st"], writes=["st"])

    def R(n):
        return [n + str(g) for g in range(8)]

    def ln_stats(src_ap_fn, srcl, use_z, zscale=1.0):
        step("ve", lambda: E["ve"].reduce_sum(
            out=st1.ap()[:, 0:1], in_=src_ap_fn(), axis=AXX),
            reads=srcl, writes=["st"])
        step("ac", lambda: E["ac"].activation(
            out=s_mt.ap(), in_=src_ap_fn(), func=AF.Square,
            accum_out=st1.ap()[:, 1:2]),
            reads=srcl, writes=R("mt") + ["st"])
        ts(2, 0, 1.0 / L)
        ts(3, 1, 1.0 / L)
        if use_z:
            step("ve", lambda: E["ve"].reciprocal(
                out=st1.ap()[:, 13:14], in_=st1.ap()[:, 12:13]),
                reads=["st"], writes=["st"])
            ts(4, 13, zscale)
            tt(5, 2, 4)
            tt(15, 3, 4)
            tt(6, 15, 4)
        else:
            ts(5, 2, 1.0)
            ts(6, 3, 1.0)
        tt(7, 5, 5)
        tt(8, 6, 7, ALU.subtract)
        ts(15, 8, EPS, ALU.add)
        step("ac", lambda: E["ac"].activation(
            out=st1.ap()[:, 14:15], in_=st1.ap()[:, 15:16], func=AF.Sqrt),
            reads=["st"], writes=["st"])
        step("ve", lambda: E["ve"].reciprocal(
            out=st1.ap()[:, 9:10], in_=st1.ap()[:, 14:15]),
            reads=["st"], writes=["st"])
        if use_z:
            tt(10, 9, 4)
        else:
            ts(10, 9, 1.0)
        tt(15, 5, 9)
        ts(11, 15, -1.0)

    def ln_apply(dst_ap_fn, dstl, src_ap_fn, srcl):
        step("ac", lambda: E["ac"].activation(
            out=dst_ap_fn(), in_=src_ap_fn(), func=AF.Identity,
            bias=st1.ap()[:, 11:12], scale=st1.ap()[:, 10:11]),
            reads=list(srcl) + ["st"], writes=dstl)

    def transpose32(src_col, src_reads, dst, dstn):
        for g in range(8):
            pt, ptn = (pt0, "pt0") if g % 2 == 0 else (pt1, "pt1")
            for q in range(4):
                mc = g * 4 + q
                step("te", lambda mc=mc, pt=pt, q=q: E["te"].transpose(
                    pt.ap()[:, q * 128:(q + 1) * 128],
                    src_col(mc), sIdent.ap()),
                    reads=src_reads(g) + ["ident"], writes=[ptn])
            cp("ve" if g % 2 == 0 else "ac",
               dst.ap()[:, g * 512:(g + 1) * 512], pt.ap(),
               reads=[ptn], writes=[dstn + str(g)])

    def fft_fwd(src, src_reads, outr, outrn, outi, outin):
        transpose32(lambda mc: src.ap()[:, mc:L:32], src_reads, s_mt, "mt")
        for g in range(8):
            pm, pmn = (pm0, "pm0") if g % 2 == 0 else (pm1, "pm1")
            for q in range(4):
                b = g * 4 + q
                step("te", lambda b=b, pm=pm, q=q: E["te"].matmul(
                    pm.ap()[:, q * 128:(q + 1) * 128],
                    sGr.ap()[:, b * 128:(b + 1) * 128],
                    s_mt.ap()[:, b * 128:(b + 1) * 128],
                    start=True, stop=True),
                    reads=["mt%d" % g, "Gr"], writes=[pmn])
            step("ve", lambda g=g, pm=pm: E["ve"].tensor_copy(
                out=s_p2r.ap().rearrange("c (rq b rp) -> c b rq rp",
                                         rq=32, b=32, rp=4)[:, g * 4:(g + 1) * 4, :, :],
                in_=pm.ap().rearrange("c (b rq rp) -> c b rq rp", b=4, rq=32, rp=4)),
                reads=[pmn], writes=R("p2r"))
            for q in range(4):
                b = g * 4 + q
                step("te", lambda b=b, pm=pm, q=q: E["te"].matmul(
                    pm.ap()[:, q * 128:(q + 1) * 128],
                    sGi.ap()[:, b * 128:(b + 1) * 128],
                    s_mt.ap()[:, b * 128:(b + 1) * 128],
                    start=True, stop=True),
                    reads=["mt%d" % g, "Gi"], writes=[pmn])
            step("ac", lambda g=g, pm=pm: E["ac"].copy(
                out=s_y2.ap().rearrange("c (rq b rp) -> c b rq rp",
                                        rq=32, b=32, rp=4)[:, g * 4:(g + 1) * 4, :, :],
                in_=pm.ap().rearrange("c (b rq rp) -> c b rq rp", b=4, rq=32, rp=4)),
                reads=[pmn], writes=R("y2"))
        transpose32(lambda rq: s_p2r.ap()[:, rq * 128:(rq + 1) * 128],
                    lambda g: ["p2r%d" % g], s_cr, "cr")
        transpose32(lambda rq: s_y2.ap()[:, rq * 128:(rq + 1) * 128],
                    lambda g: ["y2%d" % g], s_ci, "ci")
        for g in range(8):
            sl = slice(g * 512, (g + 1) * 512)
            step("te", lambda sl=sl: E["te"].matmul(
                pm0.ap(), sF2r.ap(), s_cr.ap()[:, sl], start=True, stop=False),
                reads=["cr%d" % g, "F2r"], writes=["pm0"])
            step("te", lambda sl=sl: E["te"].matmul(
                pm0.ap(), sF2ni.ap(), s_ci.ap()[:, sl], start=False, stop=True),
                reads=["ci%d" % g, "F2ni"], writes=["pm0"])
            cp("ve", outr.ap()[:, sl], pm0.ap(), reads=["pm0"],
               writes=[outrn + str(g)])
            step("te", lambda sl=sl: E["te"].matmul(
                pm1.ap(), sF2i.ap(), s_cr.ap()[:, sl], start=True, stop=False),
                reads=["cr%d" % g, "F2i"], writes=["pm1"])
            step("te", lambda sl=sl: E["te"].matmul(
                pm1.ap(), sF2r.ap(), s_ci.ap()[:, sl], start=False, stop=True),
                reads=["ci%d" % g, "F2r"], writes=["pm1"])
            cp("ac", outi.ap()[:, sl], pm1.ap(), reads=["pm1"],
               writes=[outin + str(g)])

    def fft_inv(yr, yrn, yi, yin, dst, dstn, fuse_g0=False, g0=None, g0n=None):
        cplx = yi is not None
        for g in range(8):
            sl = slice(g * 512, (g + 1) * 512)
            step("te", lambda sl=sl, cplx=cplx: E["te"].matmul(
                pm0.ap(), sFBr.ap(), yr.ap()[:, sl], start=True, stop=not cplx),
                reads=[yrn + str(g), "FBr"], writes=["pm0"])
            if cplx:
                step("te", lambda sl=sl: E["te"].matmul(
                    pm0.ap(), sFBni.ap(), yi.ap()[:, sl], start=False, stop=True),
                    reads=[yin + str(g), "FBni"], writes=["pm0"])
            cp("ve", s_mt.ap()[:, sl], pm0.ap(), reads=["pm0"],
               writes=["mt%d" % g])
            step("te", lambda sl=sl, cplx=cplx: E["te"].matmul(
                pm1.ap(), sFBi.ap(), yr.ap()[:, sl], start=True, stop=not cplx),
                reads=[yrn + str(g), "FBi"], writes=["pm1"])
            if cplx:
                step("te", lambda sl=sl: E["te"].matmul(
                    pm1.ap(), sFBr.ap(), yi.ap()[:, sl], start=False, stop=True),
                    reads=[yin + str(g), "FBr"], writes=["pm1"])
            cp("ac", s_p2r.ap()[:, sl], pm1.ap(), reads=["pm1"],
               writes=["p2r%d" % g])
        for half in range(2):
            nm, srcT, dstT, dstn2 = [("mt", s_mt, s_cr, "cr"),
                                     ("p2r", s_p2r, s_ci, "ci")][half]
            for g in range(8):
                pt, ptn = (pt0, "pt0") if g % 2 == 0 else (pt1, "pt1")
                for q in range(4):
                    rq = g * 4 + q
                    step("te", lambda rq=rq, pt=pt, q=q, srcT=srcT: E["te"].transpose(
                        pt.ap()[:, q * 128:(q + 1) * 128],
                        srcT.ap()[:, rq * 128:(rq + 1) * 128], sIdent.ap()),
                        reads=[nm + str(g), "ident"], writes=[ptn])
                eng = "ve" if g % 2 == 0 else "ac"
                o_ap = dstT.ap().rearrange("c (b r) -> c b r", b=32, r=128)[
                    :, :, g * 16:(g + 1) * 16].rearrange(
                    "c b (rq rp) -> c rq b rp", rq=4, rp=4)
                i_ap = pt.ap().rearrange("c (rq b rp) -> c rq b rp",
                                         rq=4, b=32, rp=4)
                cp(eng, o_ap, i_ap, reads=[ptn], writes=R(dstn2))
        for g in range(8):
            po, pon = (pa[0], "pa0") if g % 2 == 0 else (pa[1], "pa1")
            for q in range(4):
                bp = g * 4 + q
                step("te", lambda bp=bp, po=po, q=q: E["te"].matmul(
                    po.ap()[:, q * 128:(q + 1) * 128],
                    s_cr.ap()[:, bp * 128:(bp + 1) * 128],
                    sHr.ap()[:, bp * 128:(bp + 1) * 128],
                    start=True, stop=False),
                    reads=["cr%d" % g, "Hr"], writes=[pon])
                step("te", lambda bp=bp, po=po, q=q: E["te"].matmul(
                    po.ap()[:, q * 128:(q + 1) * 128],
                    s_ci.ap()[:, bp * 128:(bp + 1) * 128],
                    sHni.ap()[:, bp * 128:(bp + 1) * 128],
                    start=False, stop=True),
                    reads=["ci%d" % g, "Hni"], writes=[pon])
            o_ap = dst.ap().rearrange("r (a b) -> r b a", a=128, b=32)[
                :, g * 4:(g + 1) * 4, :]
            i_ap = po.ap().rearrange("r (b a) -> r b a", b=4, a=128)
            if fuse_g0:
                g_ap = g0.ap().rearrange("r (a b) -> r b a", a=128, b=32)[
                    :, g * 4:(g + 1) * 4, :]
                step("ve", lambda o=o_ap, ga=g_ap, ia=i_ap: E["ve"].scalar_tensor_tensor(
                    out=o, in0=ga, scalar=s8v.ap()[:, 0:1], in1=ia,
                    op0=ALU.mult, op1=ALU.add),
                    reads=[pon, g0n, "s8v"], writes=[dstn])
            else:
                eng = "ve" if g % 2 == 0 else "ac"
                cp(eng, o_ap, i_ap, reads=[pon], writes=[dstn])

    def vtt(out, outn, a, an, b, bn, op):
        for g in range(8):
            sl = slice(g * 512, (g + 1) * 512)
            step("ve", lambda sl=sl: E["ve"].tensor_tensor(
                out=out.ap()[:, sl], in0=a.ap()[:, sl], in1=b.ap()[:, sl],
                op=op),
                reads=[an + str(g), bn + str(g)], writes=[outn + str(g)])

    def s_const_loads():
        for k, dst in [("Gr", sGr), ("Gi", sGi), ("Hr", sHr), ("Hni", sHni)]:
            nm = {"Gr": "Gr", "Gi": "Gi", "Hr": "Hr", "Hni": "Hni"}[k]
            step("sd", lambda d=dst, s=cns[k]: E["sd"].dma_start(
                out=d.ap().rearrange("p (b c) -> p b c", b=32, c=128),
                in_=s.rearrange("b p c -> p b c")), writes=[nm], sig="cl")
        for k, dst in [("F2r", sF2r), ("F2i", sF2i), ("F2ni", sF2ni),
                       ("FBr", sFBr), ("FBi", sFBi), ("FBni", sFBni),
                       ("ident", sIdent)]:
            step("sd", lambda d=dst, s=cns[k]: E["sd"].dma_start(
                out=d.ap(), in_=s), writes=[k if k != "ident" else "ident"],
                sig="cl")
        step("sd", lambda: E["sd"].dma_start(
            out=pbs.ap(), in_=pbase_in), writes=["pbs"], sig="cl")

    def s_w_prologue():
        for jc in range(32):
            par = jc % 2
            ld, ldn = (sA, "A") if par == 0 else (sB, "B")
            wq16, wq16n = (s_ln1, lambda g: ["ln1"]) if par == 0 else                           (s_cr, lambda g: ["cr%d" % g])
            wtg, wtgn = (s_mt, "mt") if par == 0 else (s_p2r, "p2r")
            stt_t, c0, c1n, stn = (st1, 14, 15, "st") if par == 0 else \
                                  (tops, 0, 1, "tops")
            step("gd", lambda jc=jc, ld=ld: E["gd"].dma_start(
                out=ld.ap(), in_=w_in[jc * 128:(jc + 1) * 128, :]),
                writes=[ldn], sig="wl")
            step("ve", lambda jc=jc, ld=ld: E["ve"].tensor_reduce(
                out=wmax.ap()[:, jc:jc + 1], in_=ld.ap(), axis=AXX,
                op=ALU.max, apply_absolute_value=True),
                reads=[ldn], writes=["wmax"])
            step("ve", lambda jc=jc, c0=c0, stt_t=stt_t: E["ve"].reciprocal(
                out=stt_t.ap()[:, c0:c0 + 1], in_=wmax.ap()[:, jc:jc + 1]),
                reads=["wmax"], writes=[stn])
            step("ve", lambda c0=c0, c1n=c1n, stt_t=stt_t: E["ve"].tensor_scalar(
                out=stt_t.ap()[:, c1n:c1n + 1], in0=stt_t.ap()[:, c0:c0 + 1],
                scalar1=QMAX, scalar2=None, op0=ALU.mult),
                reads=[stn], writes=[stn])
            step("ac", lambda ld=ld, c1n=c1n, stt_t=stt_t: E["ac"].activation(
                out=sC.ap(), in_=ld.ap(), func=AF.Identity,
                scale=stt_t.ap()[:, c1n:c1n + 1]),
                reads=[ldn, stn], writes=["C"])
            step("ve", lambda wq16=wq16: E["ve"].tensor_scalar(
                out=wq16.ap(), in0=sC.ap(), scalar1=MAGIC, scalar2=MAGIC,
                op0=ALU.add, op1=ALU.subtract),
                reads=["C"], writes=(["ln1"] if par == 0 else R("cr")))
            transpose32(lambda mc, wq16=wq16: wq16.ap()[:, mc * 128:(mc + 1) * 128],
                        wq16n, wtg, wtgn)
            step("sd", lambda jc=jc, wtg=wtg: E["sd"].dma_start(
                out=wq_scr.rearrange("m p j -> p m j")[:, :, jc * 128:(jc + 1) * 128],
                in_=wtg.ap().rearrange("p (m j) -> p m j", m=32, j=128)),
                reads=R(wtgn), writes=["wq_scr"], sig="wqs")
        step("gd", lambda: E["gd"].dma_start(
            out=wsf_hbm.rearrange("o (a b) -> (o b) a", a=32, b=128),
            in_=wmax.ap()[:, 0:32], allow_slow_non_contiguous=True),
            reads=["wmax"], writes=["wsf_hbm"], sig="wb")
        step("gd", lambda: E["gd"].dma_start(
            out=wsf.ap()[0:1, :], in_=wsf_hbm),
            reads=["wsf_hbm"], writes=["wsf"], sig="wb")
        p = 1
        while p < 128:
            n = min(p, 128 - p)
            step("gd", lambda p=p, n=n: E["gd"].dma_start(
                out=wsf.ap()[p:p + n, :], in_=wsf.ap()[0:n, :]),
                reads=["wsf"], writes=["wsf"], sig="wb")
            p += n

    def s_attn_tile(st_i, t_i):
        tile = st_i * ST + t_i
        r0 = tile * 128
        step("sd", lambda r0=r0: E["sd"].dma_start(
            out=sA.ap(), in_=x_in[r0:r0 + 128, :]), writes=["A"], sig="xl")
        ln_stats(lambda: sA.ap(), ["A"], use_z=False)
        ln_apply(lambda: s_ln1.ap(), ["ln1"], lambda: sA.ap(), ["A"])
        ln_apply(lambda: sC.ap(), ["C"], lambda: sA.ap(), ["A"])
        step("gd", lambda: E["gd"].dma_start(
            out=lnx_scr[:, 0:L], in_=sC.ap()),
            reads=["C"], writes=["lnx"], sig="lx")
        step("gd", lambda: E["gd"].dma_start(
            out=lnx_scr[:, L:2 * L], in_=sC.ap()),
            reads=["C"], writes=["lnx"], sig="lx")
        fft_fwd(s_ln1, lambda g: ["ln1"], s_xr, "xr", s_xi, "xi")
        vtt(s_yt, "yt", s_xr, "xr", s_xr, "xr", ALU.mult)
        vtt(s_y2, "y2", s_xi, "xi", s_xi, "xi", ALU.mult)
        vtt(s_yt, "yt", s_yt, "yt", s_y2, "y2", ALU.add)
        fft_inv(s_yt, "yt", None, None, sB, "B")
        step("ve", lambda: E["ve"].max(out=tops.ap(), in_=sB.ap()),
             reads=["B"], writes=["tops"])
        step("ve", lambda: E["ve"].max_index(
            out=tid.ap(), in_max=tops.ap(), in_values=sB.ap()),
            reads=["B", "tops"], writes=["tid"])
        step("ve", lambda: E["ve"].tensor_copy(
            out=tix.ap()[:, 0:1], in_=tid.ap()[:, 0:1]),
            reads=["tid"], writes=["tix"])
        step("ve", lambda: E["ve"].tensor_tensor(
            out=tix.ap()[:, 1:2], in0=tix.ap()[:, 0:1], in1=pbs.ap(),
            op=ALU.add),
            reads=["tix", "pbs"], writes=["tix"])
        step("ve", lambda: E["ve"].scalar_tensor_tensor(
            out=sC.ap(), in0=sB.ap(), scalar=tops.ap()[:, 7:8],
            in1=sB.ap(), op0=ALU.is_ge, op1=ALU.mult),
            reads=["B", "tops"], writes=["C"])
        step("ve", lambda: E["ve"].tensor_scalar(
            out=wmax.ap()[:, 0:1], in0=tops.ap()[:, 0:1],
            scalar1=-IFS2, scalar2=None, op0=ALU.mult),
            reads=["tops"], writes=["wmax"])
        step("ac", lambda: E["ac"].activation(
            out=s8v.ap(), in_=tops.ap(), func=AF.Exp,
            bias=wmax.ap()[:, 0:1], scale=IFS2,
            accum_out=st1.ap()[:, 12:13]),
            reads=["tops", "wmax"], writes=["s8v", "st"])
        step("ac", lambda: E["ac"].activation(
            out=s_ln1.ap(), in_=sC.ap(), func=AF.Exp,
            bias=wmax.ap()[:, 0:1], scale=IFS2),
            reads=["C", "wmax"], writes=["ln1"])
        step("ve", lambda: E["ve"].scalar_tensor_tensor(
            out=s_mt.ap(), in0=s_ln1.ap(), scalar=s8v.ap()[:, 0:1],
            in1=s_ln1.ap(), op0=ALU.is_ge, op1=ALU.mult),
            reads=["ln1", "s8v"], writes=R("mt"))
        for g in range(8):
            sl = slice(g * 512, (g + 1) * 512)
            step("ve", lambda sl=sl: E["ve"].tensor_tensor(
                out=s_p2r.ap()[:, sl], in0=s_ln1.ap()[:, sl],
                in1=s_mt.ap()[:, sl], op=ALU.subtract),
                reads=["ln1", "mt%d" % g], writes=["p2r%d" % g])
        step("gd", lambda: E["gd"].indirect_dma_start(
            sC.ap(), None, lnx_scr,
            IndirectOffsetOnAxis(tix.ap()[:, 1:2], 1)),
            reads=["lnx", "tix", "C"], writes=["C"], sig="gt")
        fft_fwd(s_p2r, lambda g: R("p2r"), s_yt, "yt", s_y2, "y2")
        vtt(s_mt, "mt", s_yt, "yt", s_xr, "xr", ALU.mult)
        vtt(s_p2r, "p2r", s_y2, "y2", s_xi, "xi", ALU.mult)
        vtt(s_cr, "cr", s_yt, "yt", s_xi, "xi", ALU.mult)
        vtt(s_ci, "ci", s_y2, "y2", s_xr, "xr", ALU.mult)
        vtt(s_yt, "yt", s_mt, "mt", s_p2r, "p2r", ALU.add)
        vtt(s_y2, "y2", s_cr, "cr", s_ci, "ci", ALU.subtract)
        fft_inv(s_yt, "yt", s_y2, "y2", sB, "B",
                fuse_g0=True, g0=sC, g0n="C")
        ln_stats(lambda: sB.ap(), ["B"], use_z=True, zscale=1.0)
        ln_apply(lambda: sC.ap(), ["C"], lambda: sB.ap(), ["B"])
        step("ve", lambda: E["ve"].tensor_tensor(
            out=sB.ap(), in0=sC.ap(), in1=sA.ap(), op=ALU.add),
            reads=["C", "A"], writes=["B"])
        step("gd", lambda r0=r0: E["gd"].dma_start(
            out=x1_scr[r0:r0 + 128, :], in_=sB.ap()),
            reads=["B"], writes=["x1_scr"], sig="xs")
        ln_stats(lambda: sB.ap(), ["B"], use_z=False)
        ln_apply(lambda: sC.ap(), ["C"], lambda: sB.ap(), ["B"])
        step("ve", lambda: E["ve"].tensor_reduce(
            out=st1.ap()[:, 14:15], in_=sC.ap(), axis=AXX,
            op=ALU.max, apply_absolute_value=True),
            reads=["C"], writes=["st"])
        step("ve", lambda t_i=t_i: E["ve"].tensor_scalar(
            out=asc.ap()[:, t_i:t_i + 1], in0=st1.ap()[:, 14:15],
            scalar1=1.0 / (QMAX * QMAX), scalar2=None, op0=ALU.mult),
            reads=["st"], writes=["asc"])
        step("ve", lambda: E["ve"].reciprocal(
            out=st1.ap()[:, 15:16], in_=st1.ap()[:, 14:15]),
            reads=["st"], writes=["st"])
        ts(14, 15, QMAX)
        step("ac", lambda: E["ac"].activation(
            out=sA.ap(), in_=sC.ap(), func=AF.Identity,
            scale=st1.ap()[:, 14:15]),
            reads=["C", "st"], writes=["A"])
        step("ve", lambda: E["ve"].tensor_scalar(
            out=s_p2r.ap(), in0=sA.ap(), scalar1=MAGIC, scalar2=MAGIC,
            op0=ALU.add, op1=ALU.subtract),
            reads=["A"], writes=R("p2r"))
        for g in range(8):
            pt, ptn = (pt0, "pt0") if g % 2 == 0 else (pt1, "pt1")
            for q in range(4):
                mc = g * 4 + q
                step("te", lambda mc=mc, pt=pt, q=q: E["te"].transpose(
                    pt.ap()[:, q * 128:(q + 1) * 128],
                    s_p2r.ap()[:, mc * 128:(mc + 1) * 128], sIdent.ap()),
                    reads=["p2r%d" % g, "ident"], writes=[ptn])
            cp("ve" if g % 2 == 0 else "ac",
               xqT.ap()[:, t_i, g * 512:(g + 1) * 512], pt.ap(),
               reads=[ptn], writes=["xqT"])

    def s_ffn_supertile(st_i):
        for jq in range(8):
            stage, stagen = (s_yt, "yt") if jq % 2 == 0 else (s_y2, "y2")
            for mc in range(32):
                wslot = mc % 4
                step("sd" if mc % 2 == 0 else "gd",
                     lambda jq=jq, mc=mc, wslot=wslot: E["sd" if mc % 2 == 0 else "gd"].dma_start(
                    out=wst.ap()[:, wslot, :],
                    in_=wq_scr[mc, :, jq * 512:(jq + 1) * 512]),
                    reads=["wq_scr"], writes=["wst%d" % wslot],
                    sig="w%d" % wslot)
                for i in range(ST):
                    step("te", lambda i=i, mc=mc, wslot=wslot: E["te"].matmul(
                        pa[i].ap(),
                        xqT.ap()[:, i, mc * 128:(mc + 1) * 128],
                        wst.ap()[:, wslot, :],
                        start=(mc == 0), stop=(mc == 31)),
                        reads=["wst%d" % wslot, "xqT"], writes=["pa%d" % i])
            for i in range(ST):
                step("ve", lambda i=i, jq=jq, stage=stage: E["ve"].scalar_tensor_tensor(
                    out=stage.ap()[:, i * 512:(i + 1) * 512],
                    in0=pa[i].ap(), scalar=asc.ap()[:, i:i + 1],
                    in1=wsf.ap()[:, jq * 512:(jq + 1) * 512],
                    op0=ALU.mult, op1=ALU.mult),
                    reads=["pa%d" % i, "asc", "wsf"],
                    writes=[stagen + str(i)])
                step("sd", lambda i=i, jq=jq, st_i=st_i, stage=stage:
                     E["sd"].dma_start(
                         out=ffn_scr[st_i * ST + i, :, jq * 512:(jq + 1) * 512],
                         in_=stage.ap()[:, i * 512:(i + 1) * 512]),
                     reads=[stagen + str(i)], writes=["ffn_scr"],
                     sig="fs")

    def s_epilogue_tile(t):
        r0 = t * 128
        ftile, ftn = (s_yt, "yt") if t % 2 == 0 else (s_y2, "y2")
        xsl, xsn = (sA, "A") if t % 2 == 0 else (sC, "C")
        osl, osn = (sC, "C") if t % 2 == 0 else (sA, "A")
        step("sd", lambda t=t, ftile=ftile: E["sd"].dma_start(
            out=ftile.ap(), in_=ffn_scr[t, :, :]),
            reads=["ffn_scr"], writes=R(ftn), sig="ef")
        step("gd", lambda r0=r0, xsl=xsl: E["gd"].dma_start(
            out=xsl.ap(), in_=x1_scr[r0:r0 + 128, :]),
            reads=["x1_scr"], writes=[xsn], sig="ex")
        ln_stats(lambda ftile=ftile: ftile.ap(), R(ftn), use_z=False)
        ln_apply(lambda: sB.ap(), ["B"], lambda ftile=ftile: ftile.ap(),
                 R(ftn))
        step("ve", lambda osl=osl, xsl=xsl: E["ve"].tensor_tensor(
            out=osl.ap(), in0=sB.ap(), in1=xsl.ap(), op=ALU.add),
            reads=["B", xsn], writes=[osn])
        step("gd", lambda r0=r0, osl=osl: E["gd"].dma_start(
            out=out_t[r0:r0 + 128, :], in_=osl.ap()),
            reads=[osn], writes=["out_hbm"], sig="os")

    s_const_loads()
    s_w_prologue()
    for st_i in range(NST):
        for t_i in range(ST):
            s_attn_tile(st_i, t_i)
        s_ffn_supertile(st_i)
    for t in range(NT):
        s_epilogue_tile(t)

    counts = {n: 0 for n in sem_names}
    waited = {}
    bufstate = {}

    def BS(name):
        if name not in bufstate:
            bufstate[name] = Buf()
        return bufstate[name]

    def dummy_fn():
        return E["ve"].tensor_tensor(out=dum.ap()[:, 0:1], in0=dum.ap()[:, 1:2],
                                     in1=dum.ap()[:, 2:3], op=ALU.max)

    sched2 = []
    pre_waits = []

    def process(en, fn, reads, writes, sig):
        deps = []
        for r in reads:
            deps += BS(r).read_deps()
        for w in writes:
            deps += BS(w).write_deps()
        need = {}
        for (src, cnt) in deps:
            if src == en and en in COMPUTE:
                continue
            if cnt > waited.get((en, src), 0):
                need[src] = max(need.get(src, 0), cnt)
        wl = []
        for src, cnt in need.items():
            waited[(en, src)] = cnt
            wl.append((src, cnt))
        sched2.append((en, fn, sig))
        pre_waits.append(wl)
        counts[sig] += STEP[sig]
        tok = (sig, counts[sig])
        for r in reads:
            BS(r).note_read(tok)
        for w in writes:
            BS(w).note_write(tok)

    for (en, fn, reads, writes, sig) in sched:
        if en == "ve":
            prev_tok = ("ve", counts["ve"])
            hazard = any(BS(r).w == prev_tok for r in reads)
            if hazard:
                process("ve", dummy_fn, (), (), "ve")
        process(en, fn, reads, writes, sig)

    total = dict(counts)
    sched = sched2

    SD_SIGS = ("sd", "cl", "xl", "fs", "ef", "ex", "w0", "w1", "w2", "w3")
    GD_SIGS = ("gd", "wl", "wqs", "wb", "lx", "gt", "xs", "os")

    with nc.Block() as block:
        def emit(names, eng_obj):
            for n in names:
                E[n] = eng_obj
            for idx, (en, fn, sig) in enumerate(sched):
                if en not in names:
                    continue
                for (src, cnt) in pre_waits[idx]:
                    eng_obj.wait_ge(sems[src], cnt)
                inst = fn()
                inst.then_inc(sems[sig], STEP[sig])

        @block.tensor
        def _(te):
            emit(("te",), te)

        @block.vector
        def _(ve):
            emit(("ve",), ve)

        @block.scalar
        def _(ac):
            emit(("ac",), ac)

        @block.gpsimd
        def _(gp):
            emit(("gd",), gp)

        @block.sync
        def _(sy):
            emit(("sd",), sy)
            for s in DMA_SIGS:
                if total.get(s, 0):
                    sy.wait_ge(sems[s], total[s])

    return nc




def kernel(L0_attn_x, L0_ffn_weight):
    c = _host_consts()

    x = np.ascontiguousarray(np.asarray(L0_attn_x, dtype=np.float32)
                             .reshape(ROWS_TOTAL, L))
    W = np.ascontiguousarray(np.asarray(L0_ffn_weight, dtype=np.float32))

    nc = build_program()

    pbase = (np.arange(128, dtype=np.int32) * (2 * L)).reshape(128, 1)
    in_maps = []
    for ci in range(NCORES):
        m = {"x_in": x[ci * ROWS_CORE:(ci + 1) * ROWS_CORE], "w_in": W,
             "pbase": pbase}
        for k in ("Gr", "Gi", "F2r", "F2i", "F2ni", "FBr", "FBi", "FBni",
                  "Hr", "Hni", "ident"):
            m["c_" + k] = c[k]
        in_maps.append(m)

    res = run_bass_kernel_spmd(nc, in_maps, core_ids=list(range(NCORES)))
    out = np.empty((ROWS_TOTAL, L), np.float32)
    for ci in range(NCORES):
        out[ci * ROWS_CORE:(ci + 1) * ROWS_CORE] = res.results[ci]["out"]
    return out.reshape(B_ROWS, L, L)


if __name__ == "__main__":
    import time
    t0 = time.time()
    nc = build_program()
    print("build ok in %.1fs" % (time.time() - t0))


# version 13
# speedup vs baseline: 1.0840x; 1.0840x over previous
import numpy as np
import concourse.bass as bass
import concourse.mybir as mybir
from concourse.bass_utils import run_bass_kernel_spmd
from concourse.bass_core_rust import IndirectOffsetOnAxis

F32 = mybir.dt.float32
F16 = mybir.dt.float16
I32 = mybir.dt.int32
AF = mybir.ActivationFunctionType
ALU = mybir.AluOpType
AXX = mybir.AxisListType.X

NCORES = 8
L = 4096
B_ROWS = 4
ROWS_TOTAL = B_ROWS * L
ROWS_CORE = ROWS_TOTAL // NCORES
NT = ROWS_CORE // 128
ST = 4
NST = NT // ST
QMAX = 127.0
EPS = 1e-5
MAGIC = float(1.5 * 2 ** 23)
FS = 0.125
IFS = 1.0 / FS
IFS2 = IFS * IFS



def _host_consts():
    c = {}
    a = np.arange(128)
    b32 = np.arange(32)

    th = (np.outer(a, a) / 128.0)[None, :, :] + \
         (b32[:, None, None] * a[None, None, :] / 4096.0)
    ang = 2 * np.pi * th
    c["Gr"] = (FS * np.cos(ang)).astype(np.float16)
    c["Gi"] = (-FS * np.sin(ang)).astype(np.float16)

    bd = np.outer(b32, b32) / 32.0
    f2r = np.cos(2 * np.pi * bd)
    f2i = -np.sin(2 * np.pi * bd)

    def struct(m):
        out = np.zeros((128, 128), np.float64)
        for rp in range(4):
            out[rp::4, rp::4] = m
        return out

    c["F2r"] = struct(f2r).astype(np.float16)
    c["F2i"] = struct(f2i).astype(np.float16)
    c["F2ni"] = struct(-f2i).astype(np.float16)

    fbr = np.cos(2 * np.pi * bd)
    fbi = np.sin(2 * np.pi * bd)
    c["FBr"] = struct(fbr).astype(np.float16)
    c["FBi"] = struct(fbi).astype(np.float16)
    c["FBni"] = struct(-fbi).astype(np.float16)

    th2 = (np.outer(a, a) / 128.0)[None, :, :] + \
          (b32[:, None, None] * a[None, :, None] / 4096.0)
    ang2 = 2 * np.pi * th2
    c["Hr"] = (np.cos(ang2) / 4096.0).astype(np.float16)
    c["Hni"] = (-np.sin(ang2) / 4096.0).astype(np.float16)

    c["ident"] = np.eye(128, dtype=np.float16)
    return c




class Buf:
    __slots__ = ("w", "r")

    def __init__(self):
        self.w = None
        self.r = {}

    def read_deps(self):
        return [self.w] if self.w else []

    def write_deps(self):
        d = [(e, c) for (e, c) in self.r.items()]
        if self.w:
            d.append(self.w)
        return d

    def note_read(self, tok):
        e, c = tok
        if self.r.get(e, 0) < c:
            self.r[e] = c

    def note_write(self, tok):
        self.w = tok
        self.r = {}


def build_program():
    nc = bass.Bass("TRN2", target_bir_lowering=False)

    x_in = nc.dram_tensor("x_in", [ROWS_CORE, L], F32, kind="ExternalInput").ap()
    w_in = nc.dram_tensor("w_in", [L, L], F32, kind="ExternalInput").ap()
    out_t = nc.dram_tensor("out", [ROWS_CORE, L], F32, kind="ExternalOutput").ap()
    cns = {}
    cshapes = {"Gr": [32, 128, 128], "Gi": [32, 128, 128],
               "F2r": [128, 128], "F2i": [128, 128], "F2ni": [128, 128],
               "FBr": [128, 128], "FBi": [128, 128], "FBni": [128, 128],
               "Hr": [32, 128, 128], "Hni": [32, 128, 128],
               "ident": [128, 128]}
    for k, shp in cshapes.items():
        cns[k] = nc.dram_tensor("c_" + k, shp, F16, kind="ExternalInput").ap()
    wq_scr = nc.dram_tensor("wq_scr", [32, 128, L], F16, kind="Internal").ap()
    x1_scr = nc.dram_tensor("x1_scr", [ROWS_CORE, L], F32, kind="Internal").ap()
    ffn_scr = nc.dram_tensor("ffn_scr", [NT, 128, L], F16, kind="Internal").ap()
    wsf_hbm = nc.dram_tensor("wsf_hbm", [1, L], F32, kind="Internal").ap()
    lnx_scr = nc.dram_tensor("lnx_scr", [128, 2 * L], F32, kind="Internal").ap()
    pbase_in = nc.dram_tensor("pbase", [128, 1], I32, kind="ExternalInput").ap()

    def sb(name, shape, dtype):
        return nc.alloc_sbuf_tensor(name, shape, dtype)

    sGr = sb("sGr", [128, 32 * 128], F16)
    sGi = sb("sGi", [128, 32 * 128], F16)
    sHr = sb("sHr", [128, 32 * 128], F16)
    sHni = sb("sHni", [128, 32 * 128], F16)
    sF2r = sb("sF2r", [128, 128], F16)
    sF2i = sb("sF2i", [128, 128], F16)
    sF2ni = sb("sF2ni", [128, 128], F16)
    sFBr = sb("sFBr", [128, 128], F16)
    sFBi = sb("sFBi", [128, 128], F16)
    sFBni = sb("sFBni", [128, 128], F16)
    sIdent = sb("sIdent", [128, 128], F16)

    s_ln1 = sb("s_ln1", [128, L], F16)
    s_mt = sb("s_mt", [128, L], F16)
    s_p2r = sb("s_p2r", [128, L], F16)
    s_y2 = sb("s_y2", [128, L], F16)
    s_cr = sb("s_cr", [128, L], F16)
    s_ci = sb("s_ci", [128, L], F16)
    s_xr = sb("s_xr", [128, L], F16)
    s_xi = sb("s_xi", [128, L], F16)
    s_yt = sb("s_yt", [128, L], F16)

    sA = sb("sA", [128, L], F32)
    sB = sb("sB", [128, L], F32)
    sC = sb("sC", [128, L], F32)

    xqT = sb("xqT", [128, ST, L], F16)
    wst = sb("wst", [128, 5, 512], F16)
    wsf = sb("wsf", [128, L], F32)
    asc = sb("asc", [128, ST], F32)

    st1 = sb("st1", [128, 16], F32)
    tops = sb("tops", [128, 8], F32)
    wmax = sb("wmax", [128, 32], F32)
    dum = sb("dum", [128, 4], F32)
    s8v = sb("s8v", [128, 8], F32)
    tid = sb("tid", [128, 8], mybir.dt.uint32)
    tix = sb("tix", [128, 2], I32)
    pbs = sb("pbs", [128, 1], I32)

    pt0 = nc.alloc_psum_tensor("pt0", [128, 1024], F16)
    pt1 = nc.alloc_psum_tensor("pt1", [128, 1024], F16)
    pm0 = nc.alloc_psum_tensor("pm0", [128, 512], F32)
    pm1 = nc.alloc_psum_tensor("pm1", [128, 512], F32)
    pa = [nc.alloc_psum_tensor("pa%d" % i, [128, 512], F32) for i in range(4)]

    COMPUTE = ("te", "ve", "ac")
    DMA_SIGS = ["sd", "gd", "cl", "xl", "wl", "wqs", "wb", "lx", "gt",
                "xs", "fs", "ef", "ex", "os", "w0", "w1", "w2", "w3", "w4"]
    sem_names = list(COMPUTE) + DMA_SIGS
    sems = {n: nc.alloc_semaphore(n + "_sem") for n in sem_names}
    STEP = {n: (1 if n in COMPUTE else 16) for n in sem_names}

    sched = []

    def step(eng, fn, reads=(), writes=(), sig=None):
        sched.append((eng, fn, tuple(reads), tuple(writes),
                      sig if sig is not None else eng))

    E = {}

    def cp(eng, out_ap, in_ap, reads, writes):
        if eng == "ve":
            step("ve", lambda o=out_ap, i=in_ap: E["ve"].tensor_copy(out=o, in_=i),
                 reads=reads, writes=writes)
        else:
            step("ac", lambda o=out_ap, i=in_ap: E["ac"].copy(out=o, in_=i),
                 reads=reads, writes=writes)

    def ts(col_out, col_in, const, op=ALU.mult):
        step("ve", lambda: E["ve"].tensor_scalar(
            out=st1.ap()[:, col_out:col_out + 1],
            in0=st1.ap()[:, col_in:col_in + 1],
            scalar1=float(const), scalar2=None, op0=op),
            reads=["st"], writes=["st"])

    def tt(col_out, col_a, col_b, op=ALU.mult):
        step("ve", lambda: E["ve"].tensor_tensor(
            out=st1.ap()[:, col_out:col_out + 1],
            in0=st1.ap()[:, col_a:col_a + 1],
            in1=st1.ap()[:, col_b:col_b + 1], op=op),
            reads=["st"], writes=["st"])

    def R(n):
        return [n + str(g) for g in range(8)]

    def ln_stats(src_ap_fn, srcl, use_z, zscale=1.0):
        step("ve", lambda: E["ve"].reduce_sum(
            out=st1.ap()[:, 0:1], in_=src_ap_fn(), axis=AXX),
            reads=srcl, writes=["st"])
        step("ac", lambda: E["ac"].activation(
            out=s_mt.ap(), in_=src_ap_fn(), func=AF.Square,
            accum_out=st1.ap()[:, 1:2]),
            reads=srcl, writes=R("mt") + ["st"])
        ts(2, 0, 1.0 / L)
        ts(3, 1, 1.0 / L)
        if use_z:
            step("ve", lambda: E["ve"].reciprocal(
                out=st1.ap()[:, 13:14], in_=st1.ap()[:, 12:13]),
                reads=["st"], writes=["st"])
            ts(4, 13, zscale)
            tt(5, 2, 4)
            tt(15, 3, 4)
            tt(6, 15, 4)
        else:
            ts(5, 2, 1.0)
            ts(6, 3, 1.0)
        tt(7, 5, 5)
        tt(8, 6, 7, ALU.subtract)
        ts(15, 8, EPS, ALU.add)
        step("ac", lambda: E["ac"].activation(
            out=st1.ap()[:, 14:15], in_=st1.ap()[:, 15:16], func=AF.Sqrt),
            reads=["st"], writes=["st"])
        step("ve", lambda: E["ve"].reciprocal(
            out=st1.ap()[:, 9:10], in_=st1.ap()[:, 14:15]),
            reads=["st"], writes=["st"])
        if use_z:
            tt(10, 9, 4)
        else:
            ts(10, 9, 1.0)
        tt(15, 5, 9)
        ts(11, 15, -1.0)

    def ln_apply(dst_ap_fn, dstl, src_ap_fn, srcl):
        step("ac", lambda: E["ac"].activation(
            out=dst_ap_fn(), in_=src_ap_fn(), func=AF.Identity,
            bias=st1.ap()[:, 11:12], scale=st1.ap()[:, 10:11]),
            reads=list(srcl) + ["st"], writes=dstl)

    def transpose32(src_col, src_reads, dst, dstn):
        for h in range(4):
            pt, ptn = (pt0, "pt0") if h % 2 == 0 else (pt1, "pt1")
            for q in range(8):
                mc = h * 8 + q
                step("te", lambda mc=mc, pt=pt, q=q: E["te"].transpose(
                    pt.ap()[:, q * 128:(q + 1) * 128],
                    src_col(mc), sIdent.ap()),
                    reads=src_reads(2 * h) + src_reads(2 * h + 1) + ["ident"],
                    writes=[ptn])
            cp("ve" if h % 2 == 0 else "ac",
               dst.ap()[:, h * 1024:(h + 1) * 1024], pt.ap(),
               reads=[ptn], writes=[dstn + str(2 * h), dstn + str(2 * h + 1)])

    def fft_fwd(src, src_reads, outr, outrn, outi, outin):
        transpose32(lambda mc: src.ap()[:, mc:L:32], src_reads, s_mt, "mt")
        for g in range(8):
            pm, pmn = (pm0, "pm0") if g % 2 == 0 else (pm1, "pm1")
            for q in range(4):
                b = g * 4 + q
                step("te", lambda b=b, pm=pm, q=q: E["te"].matmul(
                    pm.ap()[:, q * 128:(q + 1) * 128],
                    sGr.ap()[:, b * 128:(b + 1) * 128],
                    s_mt.ap()[:, b * 128:(b + 1) * 128],
                    start=True, stop=True),
                    reads=["mt%d" % g, "Gr"], writes=[pmn])
            step("ve", lambda g=g, pm=pm: E["ve"].tensor_copy(
                out=s_p2r.ap().rearrange("c (rq b rp) -> c b rq rp",
                                         rq=32, b=32, rp=4)[:, g * 4:(g + 1) * 4, :, :],
                in_=pm.ap().rearrange("c (b rq rp) -> c b rq rp", b=4, rq=32, rp=4)),
                reads=[pmn], writes=R("p2r"))
            for q in range(4):
                b = g * 4 + q
                step("te", lambda b=b, pm=pm, q=q: E["te"].matmul(
                    pm.ap()[:, q * 128:(q + 1) * 128],
                    sGi.ap()[:, b * 128:(b + 1) * 128],
                    s_mt.ap()[:, b * 128:(b + 1) * 128],
                    start=True, stop=True),
                    reads=["mt%d" % g, "Gi"], writes=[pmn])
            step("ac", lambda g=g, pm=pm: E["ac"].copy(
                out=s_y2.ap().rearrange("c (rq b rp) -> c b rq rp",
                                        rq=32, b=32, rp=4)[:, g * 4:(g + 1) * 4, :, :],
                in_=pm.ap().rearrange("c (b rq rp) -> c b rq rp", b=4, rq=32, rp=4)),
                reads=[pmn], writes=R("y2"))
        transpose32(lambda rq: s_p2r.ap()[:, rq * 128:(rq + 1) * 128],
                    lambda g: ["p2r%d" % g], s_cr, "cr")
        transpose32(lambda rq: s_y2.ap()[:, rq * 128:(rq + 1) * 128],
                    lambda g: ["y2%d" % g], s_ci, "ci")
        for g in range(8):
            sl = slice(g * 512, (g + 1) * 512)
            step("te", lambda sl=sl: E["te"].matmul(
                pm0.ap(), sF2r.ap(), s_cr.ap()[:, sl], start=True, stop=False),
                reads=["cr%d" % g, "F2r"], writes=["pm0"])
            step("te", lambda sl=sl: E["te"].matmul(
                pm0.ap(), sF2ni.ap(), s_ci.ap()[:, sl], start=False, stop=True),
                reads=["ci%d" % g, "F2ni"], writes=["pm0"])
            cp("ve", outr.ap()[:, sl], pm0.ap(), reads=["pm0"],
               writes=[outrn + str(g)])
            step("te", lambda sl=sl: E["te"].matmul(
                pm1.ap(), sF2i.ap(), s_cr.ap()[:, sl], start=True, stop=False),
                reads=["cr%d" % g, "F2i"], writes=["pm1"])
            step("te", lambda sl=sl: E["te"].matmul(
                pm1.ap(), sF2r.ap(), s_ci.ap()[:, sl], start=False, stop=True),
                reads=["ci%d" % g, "F2r"], writes=["pm1"])
            cp("ac", outi.ap()[:, sl], pm1.ap(), reads=["pm1"],
               writes=[outin + str(g)])

    def fft_inv(yr, yrn, yi, yin, dst, dstn, fuse_g0=False, g0=None, g0n=None):
        cplx = yi is not None
        for g in range(8):
            sl = slice(g * 512, (g + 1) * 512)
            step("te", lambda sl=sl, cplx=cplx: E["te"].matmul(
                pm0.ap(), sFBr.ap(), yr.ap()[:, sl], start=True, stop=not cplx),
                reads=[yrn + str(g), "FBr"], writes=["pm0"])
            if cplx:
                step("te", lambda sl=sl: E["te"].matmul(
                    pm0.ap(), sFBni.ap(), yi.ap()[:, sl], start=False, stop=True),
                    reads=[yin + str(g), "FBni"], writes=["pm0"])
            cp("ve", s_mt.ap()[:, sl], pm0.ap(), reads=["pm0"],
               writes=["mt%d" % g])
            step("te", lambda sl=sl, cplx=cplx: E["te"].matmul(
                pm1.ap(), sFBi.ap(), yr.ap()[:, sl], start=True, stop=not cplx),
                reads=[yrn + str(g), "FBi"], writes=["pm1"])
            if cplx:
                step("te", lambda sl=sl: E["te"].matmul(
                    pm1.ap(), sFBr.ap(), yi.ap()[:, sl], start=False, stop=True),
                    reads=[yin + str(g), "FBr"], writes=["pm1"])
            cp("ac", s_p2r.ap()[:, sl], pm1.ap(), reads=["pm1"],
               writes=["p2r%d" % g])
        for half in range(2):
            nm, srcT, dstT, dstn2 = [("mt", s_mt, s_cr, "cr"),
                                     ("p2r", s_p2r, s_ci, "ci")][half]
            for h in range(4):
                pt, ptn = (pt0, "pt0") if h % 2 == 0 else (pt1, "pt1")
                for q in range(8):
                    rq = h * 8 + q
                    step("te", lambda rq=rq, pt=pt, q=q, srcT=srcT: E["te"].transpose(
                        pt.ap()[:, q * 128:(q + 1) * 128],
                        srcT.ap()[:, rq * 128:(rq + 1) * 128], sIdent.ap()),
                        reads=[nm + str(rq // 4), "ident"], writes=[ptn])
                eng = "ve" if h % 2 == 0 else "ac"
                o_ap = dstT.ap().rearrange("c (b r) -> c b r", b=32, r=128)[
                    :, :, h * 32:(h + 1) * 32].rearrange(
                    "c b (rq rp) -> c rq b rp", rq=8, rp=4)
                i_ap = pt.ap().rearrange("c (rq b rp) -> c rq b rp",
                                         rq=8, b=32, rp=4)
                cp(eng, o_ap, i_ap, reads=[ptn], writes=R(dstn2))
        for g in range(8):
            po, pon = (pa[0], "pa0") if g % 2 == 0 else (pa[1], "pa1")
            for q in range(4):
                bp = g * 4 + q
                step("te", lambda bp=bp, po=po, q=q: E["te"].matmul(
                    po.ap()[:, q * 128:(q + 1) * 128],
                    s_cr.ap()[:, bp * 128:(bp + 1) * 128],
                    sHr.ap()[:, bp * 128:(bp + 1) * 128],
                    start=True, stop=False),
                    reads=["cr%d" % g, "Hr"], writes=[pon])
                step("te", lambda bp=bp, po=po, q=q: E["te"].matmul(
                    po.ap()[:, q * 128:(q + 1) * 128],
                    s_ci.ap()[:, bp * 128:(bp + 1) * 128],
                    sHni.ap()[:, bp * 128:(bp + 1) * 128],
                    start=False, stop=True),
                    reads=["ci%d" % g, "Hni"], writes=[pon])
            o_ap = dst.ap().rearrange("r (a b) -> r b a", a=128, b=32)[
                :, g * 4:(g + 1) * 4, :]
            i_ap = po.ap().rearrange("r (b a) -> r b a", b=4, a=128)
            if fuse_g0:
                g_ap = g0.ap().rearrange("r (a b) -> r b a", a=128, b=32)[
                    :, g * 4:(g + 1) * 4, :]
                step("ve", lambda o=o_ap, ga=g_ap, ia=i_ap: E["ve"].scalar_tensor_tensor(
                    out=o, in0=ga, scalar=s8v.ap()[:, 0:1], in1=ia,
                    op0=ALU.mult, op1=ALU.add),
                    reads=[pon, g0n, "s8v"], writes=[dstn])
            else:
                eng = "ve" if g % 2 == 0 else "ac"
                cp(eng, o_ap, i_ap, reads=[pon], writes=[dstn])

    def vtt(out, outn, a, an, b, bn, op):
        for g in range(8):
            sl = slice(g * 512, (g + 1) * 512)
            step("ve", lambda sl=sl: E["ve"].tensor_tensor(
                out=out.ap()[:, sl], in0=a.ap()[:, sl], in1=b.ap()[:, sl],
                op=op),
                reads=[an + str(g), bn + str(g)], writes=[outn + str(g)])

    def s_const_loads():
        for k, dst in [("Gr", sGr), ("Gi", sGi), ("Hr", sHr), ("Hni", sHni)]:
            nm = {"Gr": "Gr", "Gi": "Gi", "Hr": "Hr", "Hni": "Hni"}[k]
            step("sd", lambda d=dst, s=cns[k]: E["sd"].dma_start(
                out=d.ap().rearrange("p (b c) -> p b c", b=32, c=128),
                in_=s.rearrange("b p c -> p b c")), writes=[nm], sig="cl")
        for k, dst in [("F2r", sF2r), ("F2i", sF2i), ("F2ni", sF2ni),
                       ("FBr", sFBr), ("FBi", sFBi), ("FBni", sFBni),
                       ("ident", sIdent)]:
            step("sd", lambda d=dst, s=cns[k]: E["sd"].dma_start(
                out=d.ap(), in_=s), writes=[k if k != "ident" else "ident"],
                sig="cl")
        step("sd", lambda: E["sd"].dma_start(
            out=pbs.ap(), in_=pbase_in), writes=["pbs"], sig="cl")

    def s_w_prologue():
        for jc in range(32):
            par = jc % 2
            ld, ldn = (sA, "A") if par == 0 else (sB, "B")
            wq16, wq16n = (s_ln1, lambda g: ["ln1"]) if par == 0 else                           (s_cr, lambda g: ["cr%d" % g])
            wtg, wtgn = (s_mt, "mt") if par == 0 else (s_p2r, "p2r")
            stt_t, c0, c1n, stn = (st1, 14, 15, "st") if par == 0 else \
                                  (tops, 0, 1, "tops")
            step("gd", lambda jc=jc, ld=ld: E["gd"].dma_start(
                out=ld.ap(), in_=w_in[jc * 128:(jc + 1) * 128, :]),
                writes=[ldn], sig="wl")
            step("ve", lambda jc=jc, ld=ld: E["ve"].tensor_reduce(
                out=wmax.ap()[:, jc:jc + 1], in_=ld.ap(), axis=AXX,
                op=ALU.max, apply_absolute_value=True),
                reads=[ldn], writes=["wmax"])
            step("ve", lambda jc=jc, c0=c0, stt_t=stt_t: E["ve"].reciprocal(
                out=stt_t.ap()[:, c0:c0 + 1], in_=wmax.ap()[:, jc:jc + 1]),
                reads=["wmax"], writes=[stn])
            step("ve", lambda c0=c0, c1n=c1n, stt_t=stt_t: E["ve"].tensor_scalar(
                out=stt_t.ap()[:, c1n:c1n + 1], in0=stt_t.ap()[:, c0:c0 + 1],
                scalar1=QMAX, scalar2=None, op0=ALU.mult),
                reads=[stn], writes=[stn])
            step("ac", lambda ld=ld, c1n=c1n, stt_t=stt_t: E["ac"].activation(
                out=sC.ap(), in_=ld.ap(), func=AF.Identity,
                scale=stt_t.ap()[:, c1n:c1n + 1]),
                reads=[ldn, stn], writes=["C"])
            step("ve", lambda wq16=wq16: E["ve"].tensor_scalar(
                out=wq16.ap(), in0=sC.ap(), scalar1=MAGIC, scalar2=MAGIC,
                op0=ALU.add, op1=ALU.subtract),
                reads=["C"], writes=(["ln1"] if par == 0 else R("cr")))
            transpose32(lambda mc, wq16=wq16: wq16.ap()[:, mc * 128:(mc + 1) * 128],
                        wq16n, wtg, wtgn)
            step("sd", lambda jc=jc, wtg=wtg: E["sd"].dma_start(
                out=wq_scr.rearrange("m p j -> p m j")[:, :, jc * 128:(jc + 1) * 128],
                in_=wtg.ap().rearrange("p (m j) -> p m j", m=32, j=128)),
                reads=R(wtgn), writes=["wq_scr"], sig="wqs")
        step("gd", lambda: E["gd"].dma_start(
            out=wsf_hbm.rearrange("o (a b) -> (o b) a", a=32, b=128),
            in_=wmax.ap()[:, 0:32], allow_slow_non_contiguous=True),
            reads=["wmax"], writes=["wsf_hbm"], sig="wb")
        step("gd", lambda: E["gd"].dma_start(
            out=wsf.ap()[0:1, :], in_=wsf_hbm),
            reads=["wsf_hbm"], writes=["wsf"], sig="wb")
        p = 1
        while p < 128:
            n = min(p, 128 - p)
            step("gd", lambda p=p, n=n: E["gd"].dma_start(
                out=wsf.ap()[p:p + n, :], in_=wsf.ap()[0:n, :]),
                reads=["wsf"], writes=["wsf"], sig="wb")
            p += n

    def s_attn_tile(st_i, t_i):
        tile = st_i * ST + t_i
        r0 = tile * 128
        step("sd", lambda r0=r0: E["sd"].dma_start(
            out=sA.ap(), in_=x_in[r0:r0 + 128, :]), writes=["A"], sig="xl")
        ln_stats(lambda: sA.ap(), ["A"], use_z=False)
        ln_apply(lambda: s_ln1.ap(), ["ln1"], lambda: sA.ap(), ["A"])
        ln_apply(lambda: sC.ap(), ["C"], lambda: sA.ap(), ["A"])
        step("gd", lambda: E["gd"].dma_start(
            out=lnx_scr[:, 0:L], in_=sC.ap()),
            reads=["C"], writes=["lnx"], sig="lx")
        step("gd", lambda: E["gd"].dma_start(
            out=lnx_scr[:, L:2 * L], in_=sC.ap()),
            reads=["C"], writes=["lnx"], sig="lx")
        fft_fwd(s_ln1, lambda g: ["ln1"], s_xr, "xr", s_xi, "xi")
        vtt(s_yt, "yt", s_xr, "xr", s_xr, "xr", ALU.mult)
        vtt(s_y2, "y2", s_xi, "xi", s_xi, "xi", ALU.mult)
        vtt(s_yt, "yt", s_yt, "yt", s_y2, "y2", ALU.add)
        fft_inv(s_yt, "yt", None, None, sB, "B")
        step("ve", lambda: E["ve"].max(out=tops.ap(), in_=sB.ap()),
             reads=["B"], writes=["tops"])
        step("ve", lambda: E["ve"].max_index(
            out=tid.ap(), in_max=tops.ap(), in_values=sB.ap()),
            reads=["B", "tops"], writes=["tid"])
        step("ve", lambda: E["ve"].tensor_copy(
            out=tix.ap()[:, 0:1], in_=tid.ap()[:, 0:1]),
            reads=["tid"], writes=["tix"])
        step("ve", lambda: E["ve"].tensor_tensor(
            out=tix.ap()[:, 1:2], in0=tix.ap()[:, 0:1], in1=pbs.ap(),
            op=ALU.add),
            reads=["tix", "pbs"], writes=["tix"])
        step("ve", lambda: E["ve"].scalar_tensor_tensor(
            out=sC.ap(), in0=sB.ap(), scalar=tops.ap()[:, 7:8],
            in1=sB.ap(), op0=ALU.is_ge, op1=ALU.mult),
            reads=["B", "tops"], writes=["C"])
        step("ve", lambda: E["ve"].tensor_scalar(
            out=wmax.ap()[:, 0:1], in0=tops.ap()[:, 0:1],
            scalar1=-IFS2, scalar2=None, op0=ALU.mult),
            reads=["tops"], writes=["wmax"])
        step("ac", lambda: E["ac"].activation(
            out=s8v.ap(), in_=tops.ap(), func=AF.Exp,
            bias=wmax.ap()[:, 0:1], scale=IFS2,
            accum_out=st1.ap()[:, 12:13]),
            reads=["tops", "wmax"], writes=["s8v", "st"])
        step("ac", lambda: E["ac"].activation(
            out=s_ln1.ap(), in_=sC.ap(), func=AF.Exp,
            bias=wmax.ap()[:, 0:1], scale=IFS2),
            reads=["C", "wmax"], writes=["ln1"])
        step("ve", lambda: E["ve"].scalar_tensor_tensor(
            out=s_mt.ap(), in0=s_ln1.ap(), scalar=s8v.ap()[:, 0:1],
            in1=s_ln1.ap(), op0=ALU.is_ge, op1=ALU.mult),
            reads=["ln1", "s8v"], writes=R("mt"))
        for g in range(8):
            sl = slice(g * 512, (g + 1) * 512)
            step("ve", lambda sl=sl: E["ve"].tensor_tensor(
                out=s_p2r.ap()[:, sl], in0=s_ln1.ap()[:, sl],
                in1=s_mt.ap()[:, sl], op=ALU.subtract),
                reads=["ln1", "mt%d" % g], writes=["p2r%d" % g])
        step("gd", lambda: E["gd"].indirect_dma_start(
            sC.ap(), None, lnx_scr,
            IndirectOffsetOnAxis(tix.ap()[:, 1:2], 1)),
            reads=["lnx", "tix", "C"], writes=["C"], sig="gt")
        fft_fwd(s_p2r, lambda g: R("p2r"), s_yt, "yt", s_y2, "y2")
        vtt(s_mt, "mt", s_yt, "yt", s_xr, "xr", ALU.mult)
        vtt(s_p2r, "p2r", s_y2, "y2", s_xi, "xi", ALU.mult)
        vtt(s_cr, "cr", s_yt, "yt", s_xi, "xi", ALU.mult)
        vtt(s_ci, "ci", s_y2, "y2", s_xr, "xr", ALU.mult)
        vtt(s_yt, "yt", s_mt, "mt", s_p2r, "p2r", ALU.add)
        vtt(s_y2, "y2", s_cr, "cr", s_ci, "ci", ALU.subtract)
        fft_inv(s_yt, "yt", s_y2, "y2", sB, "B",
                fuse_g0=True, g0=sC, g0n="C")
        ln_stats(lambda: sB.ap(), ["B"], use_z=True, zscale=1.0)
        ln_apply(lambda: sC.ap(), ["C"], lambda: sB.ap(), ["B"])
        step("ve", lambda: E["ve"].tensor_tensor(
            out=sB.ap(), in0=sC.ap(), in1=sA.ap(), op=ALU.add),
            reads=["C", "A"], writes=["B"])
        step("gd", lambda r0=r0: E["gd"].dma_start(
            out=x1_scr[r0:r0 + 128, :], in_=sB.ap()),
            reads=["B"], writes=["x1_scr"], sig="xs")
        ln_stats(lambda: sB.ap(), ["B"], use_z=False)
        ln_apply(lambda: sC.ap(), ["C"], lambda: sB.ap(), ["B"])
        step("ve", lambda: E["ve"].tensor_reduce(
            out=st1.ap()[:, 14:15], in_=sC.ap(), axis=AXX,
            op=ALU.max, apply_absolute_value=True),
            reads=["C"], writes=["st"])
        step("ve", lambda t_i=t_i: E["ve"].tensor_scalar(
            out=asc.ap()[:, t_i:t_i + 1], in0=st1.ap()[:, 14:15],
            scalar1=1.0 / (QMAX * QMAX), scalar2=None, op0=ALU.mult),
            reads=["st"], writes=["asc"])
        step("ve", lambda: E["ve"].reciprocal(
            out=st1.ap()[:, 15:16], in_=st1.ap()[:, 14:15]),
            reads=["st"], writes=["st"])
        ts(14, 15, QMAX)
        step("ac", lambda: E["ac"].activation(
            out=sA.ap(), in_=sC.ap(), func=AF.Identity,
            scale=st1.ap()[:, 14:15]),
            reads=["C", "st"], writes=["A"])
        step("ve", lambda: E["ve"].tensor_scalar(
            out=s_p2r.ap(), in0=sA.ap(), scalar1=MAGIC, scalar2=MAGIC,
            op0=ALU.add, op1=ALU.subtract),
            reads=["A"], writes=R("p2r"))
        for h in range(4):
            pt, ptn = (pt0, "pt0") if h % 2 == 0 else (pt1, "pt1")
            for q in range(8):
                mc = h * 8 + q
                step("te", lambda mc=mc, pt=pt, q=q: E["te"].transpose(
                    pt.ap()[:, q * 128:(q + 1) * 128],
                    s_p2r.ap()[:, mc * 128:(mc + 1) * 128], sIdent.ap()),
                    reads=["p2r%d" % (mc // 4), "ident"], writes=[ptn])
            cp("ve" if h % 2 == 0 else "ac",
               xqT.ap()[:, t_i, h * 1024:(h + 1) * 1024], pt.ap(),
               reads=[ptn], writes=["xqT"])

    def s_ffn_supertile(st_i):
        for jq in range(8):
            stage, stagen = (s_yt, "yt") if jq % 2 == 0 else (s_y2, "y2")
            for mc in range(32):
                wslot = mc % 5
                step("sd" if mc % 2 == 0 else "gd",
                     lambda jq=jq, mc=mc, wslot=wslot: E["sd" if mc % 2 == 0 else "gd"].dma_start(
                    out=wst.ap()[:, wslot, :],
                    in_=wq_scr[mc, :, jq * 512:(jq + 1) * 512]),
                    reads=["wq_scr"], writes=["wst%d" % wslot],
                    sig="w%d" % wslot)
                for i in range(ST):
                    step("te", lambda i=i, mc=mc, wslot=wslot: E["te"].matmul(
                        pa[i].ap(),
                        xqT.ap()[:, i, mc * 128:(mc + 1) * 128],
                        wst.ap()[:, wslot, :],
                        start=(mc == 0), stop=(mc == 31)),
                        reads=["wst%d" % wslot, "xqT"], writes=["pa%d" % i])
            for i in range(ST):
                step("ve", lambda i=i, jq=jq, stage=stage: E["ve"].scalar_tensor_tensor(
                    out=stage.ap()[:, i * 512:(i + 1) * 512],
                    in0=pa[i].ap(), scalar=asc.ap()[:, i:i + 1],
                    in1=wsf.ap()[:, jq * 512:(jq + 1) * 512],
                    op0=ALU.mult, op1=ALU.mult),
                    reads=["pa%d" % i, "asc", "wsf"],
                    writes=[stagen + str(i)])
                step("sd", lambda i=i, jq=jq, st_i=st_i, stage=stage:
                     E["sd"].dma_start(
                         out=ffn_scr[st_i * ST + i, :, jq * 512:(jq + 1) * 512],
                         in_=stage.ap()[:, i * 512:(i + 1) * 512]),
                     reads=[stagen + str(i)], writes=["ffn_scr"],
                     sig="fs")

    def s_epilogue_tile(t):
        r0 = t * 128
        ftile, ftn = (s_yt, "yt") if t % 2 == 0 else (s_y2, "y2")
        xsl, xsn = (sA, "A") if t % 2 == 0 else (sC, "C")
        osl, osn = (sC, "C") if t % 2 == 0 else (sA, "A")
        step("sd", lambda t=t, ftile=ftile: E["sd"].dma_start(
            out=ftile.ap(), in_=ffn_scr[t, :, :]),
            reads=["ffn_scr"], writes=R(ftn), sig="ef")
        step("gd", lambda r0=r0, xsl=xsl: E["gd"].dma_start(
            out=xsl.ap(), in_=x1_scr[r0:r0 + 128, :]),
            reads=["x1_scr"], writes=[xsn], sig="ex")
        ln_stats(lambda ftile=ftile: ftile.ap(), R(ftn), use_z=False)
        ln_apply(lambda: sB.ap(), ["B"], lambda ftile=ftile: ftile.ap(),
                 R(ftn))
        step("ve", lambda osl=osl, xsl=xsl: E["ve"].tensor_tensor(
            out=osl.ap(), in0=sB.ap(), in1=xsl.ap(), op=ALU.add),
            reads=["B", xsn], writes=[osn])
        step("gd", lambda r0=r0, osl=osl: E["gd"].dma_start(
            out=out_t[r0:r0 + 128, :], in_=osl.ap()),
            reads=[osn], writes=["out_hbm"], sig="os")

    s_const_loads()
    s_w_prologue()
    for st_i in range(NST):
        for t_i in range(ST):
            s_attn_tile(st_i, t_i)
        s_ffn_supertile(st_i)
    for t in range(NT):
        s_epilogue_tile(t)

    counts = {n: 0 for n in sem_names}
    waited = {}
    bufstate = {}

    def BS(name):
        if name not in bufstate:
            bufstate[name] = Buf()
        return bufstate[name]

    def dummy_fn():
        return E["ve"].tensor_tensor(out=dum.ap()[:, 0:1], in0=dum.ap()[:, 1:2],
                                     in1=dum.ap()[:, 2:3], op=ALU.max)

    sched2 = []
    pre_waits = []

    def process(en, fn, reads, writes, sig):
        deps = []
        for r in reads:
            deps += BS(r).read_deps()
        for w in writes:
            deps += BS(w).write_deps()
        need = {}
        for (src, cnt) in deps:
            if src == en and en in COMPUTE:
                continue
            if cnt > waited.get((en, src), 0):
                need[src] = max(need.get(src, 0), cnt)
        wl = []
        for src, cnt in need.items():
            waited[(en, src)] = cnt
            wl.append((src, cnt))
        sched2.append((en, fn, sig))
        pre_waits.append(wl)
        counts[sig] += STEP[sig]
        tok = (sig, counts[sig])
        for r in reads:
            BS(r).note_read(tok)
        for w in writes:
            BS(w).note_write(tok)

    for (en, fn, reads, writes, sig) in sched:
        if en == "ve":
            prev_tok = ("ve", counts["ve"])
            hazard = any(BS(r).w == prev_tok for r in reads)
            if hazard:
                process("ve", dummy_fn, (), (), "ve")
        process(en, fn, reads, writes, sig)

    total = dict(counts)
    sched = sched2

    SD_SIGS = ("sd", "cl", "xl", "fs", "ef", "ex", "w0", "w1", "w2", "w3")
    GD_SIGS = ("gd", "wl", "wqs", "wb", "lx", "gt", "xs", "os")

    with nc.Block() as block:
        def emit(names, eng_obj):
            for n in names:
                E[n] = eng_obj
            for idx, (en, fn, sig) in enumerate(sched):
                if en not in names:
                    continue
                for (src, cnt) in pre_waits[idx]:
                    eng_obj.wait_ge(sems[src], cnt)
                inst = fn()
                inst.then_inc(sems[sig], STEP[sig])

        @block.tensor
        def _(te):
            emit(("te",), te)

        @block.vector
        def _(ve):
            emit(("ve",), ve)

        @block.scalar
        def _(ac):
            emit(("ac",), ac)

        @block.gpsimd
        def _(gp):
            emit(("gd",), gp)

        @block.sync
        def _(sy):
            emit(("sd",), sy)
            for s in DMA_SIGS:
                if total.get(s, 0):
                    sy.wait_ge(sems[s], total[s])

    return nc




def kernel(L0_attn_x, L0_ffn_weight):
    c = _host_consts()

    x = np.ascontiguousarray(np.asarray(L0_attn_x, dtype=np.float32)
                             .reshape(ROWS_TOTAL, L))
    W = np.ascontiguousarray(np.asarray(L0_ffn_weight, dtype=np.float32))

    nc = build_program()

    pbase = (np.arange(128, dtype=np.int32) * (2 * L)).reshape(128, 1)
    in_maps = []
    for ci in range(NCORES):
        m = {"x_in": x[ci * ROWS_CORE:(ci + 1) * ROWS_CORE], "w_in": W,
             "pbase": pbase}
        for k in ("Gr", "Gi", "F2r", "F2i", "F2ni", "FBr", "FBi", "FBni",
                  "Hr", "Hni", "ident"):
            m["c_" + k] = c[k]
        in_maps.append(m)

    res = run_bass_kernel_spmd(nc, in_maps, core_ids=list(range(NCORES)))
    out = np.empty((ROWS_TOTAL, L), np.float32)
    for ci in range(NCORES):
        out[ci * ROWS_CORE:(ci + 1) * ROWS_CORE] = res.results[ci]["out"]
    return out.reshape(B_ROWS, L, L)


if __name__ == "__main__":
    import time
    t0 = time.time()
    nc = build_program()
    print("build ok in %.1fs" % (time.time() - t0))
